# version 02
# baseline (speedup 1.0000x reference)
import math
import numpy as np
from contextlib import ExitStack

import concourse.bass as bass
import concourse.mybir as mybir
from concourse.bass_utils import run_bass_kernel_spmd

F32 = mybir.dt.float32
I32 = mybir.dt.int32
AF = mybir.ActivationFunctionType
ALU = mybir.AluOpType
AX = mybir.AxisListType

B, S, D = 2, 4096, 4096
TPC = 1024
KKEEP = 410.0
RSQ = 1.0 / math.sqrt(128.0)
NEG = -1.0e9
NBIS = 22
NF4 = [-1.0, -0.6961928009986877, -0.5250730514526367, -0.39491748809814453,
       -0.28444138169288635, -0.18477343022823334, -0.09105003625154495, 0.0,
       0.07958029955625534, 0.16093020141124725, 0.24611230194568634,
       0.33791524171829224, 0.44070982933044434, 0.5626170039176941,
       0.7229568362236023, 1.0]


class Ctx:
    ENGS = ("t", "v", "s", "g", "y")

    def __init__(self, nc):
        self.nc = nc
        self.ops = []
        self.cnt = {}
        self.sems = {}

    ESEM = {"v": "V", "s": "S"}

    def sem(self, *names):
        for n in names:
            self.cnt[n] = 0

    def step(self, eng, fn, inc=None, wait=None):
        wait = dict(wait or {})
        es_ = self.ESEM.get(eng)
        if es_ is not None and self.cnt.get(es_, 0) > 0:
            wait.setdefault(es_, self.cnt[es_])
        self.ops.append((eng, wait, fn, dict(inc or {})))
        for n, a in (inc or {}).items():
            self.cnt[n] += a

    def w(self, *names):
        return {n: self.cnt[n] for n in names}

    def c(self, name):
        return self.cnt[name]

    def replay(self, block):
        def mk(engname):
            def run(eng):
                hi = {}
                for e, waits, fn, incs in self.ops:
                    if e != engname:
                        continue
                    for n, vv in waits.items():
                        if vv > 0 and hi.get(n, -1) < vv:
                            eng.wait_ge(self.sems[n], vv)
                            hi[n] = vv
                    ins = fn(eng)
                    if incs:
                        assert ins is not None
                        for n, a in incs.items():
                            ins = ins.then_inc(self.sems[n], a)
            return run
        block.tensor(mk("t"))
        block.vector(mk("v"))
        block.scalar(mk("s"))
        block.gpsimd(mk("g"))
        block.sync(mk("y"))


def finish(nc, C, es):
    for name in list(C.cnt):
        C.sems[name] = es.enter_context(nc.semaphore(name))
    with nc.Block() as block:
        C.replay(block)


def _rope_tabs(layer, positions):
    inv = (1.0 / (10000.0 ** (np.arange(0, 128, 2, dtype=np.float32) / np.float32(128)))).astype(np.float32)
    pos = positions.astype(np.float32)
    ang = (pos[:, None] * inv[None, :]).astype(np.float32)
    if layer == 0:
        emb = np.concatenate([ang, ang], -1)
        return (np.ascontiguousarray(np.cos(emb).T.astype(np.float32)),
                np.ascontiguousarray(np.sin(emb).T.astype(np.float32)))
    c = np.repeat(np.cos(ang).astype(np.float32), 2, axis=1)
    sn = np.sin(ang).astype(np.float32)
    s_ = np.empty((len(positions), 128), np.float32)
    s_[:, 0::2] = sn
    s_[:, 1::2] = sn
    return (np.ascontiguousarray(c.T.astype(np.float32)),
            np.ascontiguousarray(s_.T.astype(np.float32)))


def _perm(layer):
    P = np.zeros((128, 128), np.float32)
    if layer == 0:
        for m in range(64):
            P[m + 64, m] = -1.0
            P[m, m + 64] = 1.0
    else:
        for m in range(0, 128, 2):
            P[m + 1, m] = -1.0
            P[m, m + 1] = 1.0
    return P


def _diagneg():
    M = np.zeros((128, 128), np.float32)
    for q in range(128):
        M[q, q + 1:] = NEG
    return M


def _sel2():
    s2 = np.zeros((2, 128), np.float32)
    s2[0, 0:64] = 1.0
    s2[1, 64:128] = 1.0
    return s2


def _vismask(core):
    vl = int(S * 0.3)
    r = core % 4
    m = np.zeros((128, 16), np.float32)
    for tt in range(8):
        pos = 1024 * r + 128 * tt + np.arange(128)
        m[:, 2 * tt] = (pos < vl)
        m[:, 2 * tt + 1] = (pos >= vl)
    return m


IDENT = np.eye(128, dtype=np.float32)


def emit_ln(C, XT, XLN, cb, ld_wait):
    v, s = "v", "s"
    c0, c1, c2 = cb[:, 0:1], cb[:, 1:2], cb[:, 2:3]
    C.step(v, lambda e: e.tensor_reduce(out=c0, in_=XT[:], axis=AX.X, op=ALU.add),
           inc={"V": 1}, wait=ld_wait)
    C.step(v, lambda e: e.tensor_scalar(out=c0, in0=c0, scalar1=-1.0 / 4096,
                                        scalar2=None, op0=ALU.mult), inc={"V": 1})
    C.step(s, lambda e: e.activation(out=XLN[:], in_=XT[:], func=AF.Square,
                                     accum_out=c1), inc={"S": 1},
           wait={**ld_wait, "V": C.c("V")})
    C.step(v, lambda e: e.tensor_scalar(out=c1, in0=c1, scalar1=1.0 / 4096,
                                        scalar2=None, op0=ALU.mult),
           inc={"V": 1}, wait=C.w("S"))
    C.step(v, lambda e: e.tensor_mul(out=c2, in0=c0, in1=c0), inc={"V": 1})
    C.step(v, lambda e: e.scalar_tensor_tensor(out=c1, in0=c1, scalar=1e-5,
                                               in1=c2, op0=ALU.add,
                                               op1=ALU.subtract), inc={"V": 1})
    C.step(s, lambda e: e.activation(out=c2, in_=c1, func=AF.Ln), inc={"S": 1},
           wait=C.w("V"))
    C.step(s, lambda e: e.activation(out=c2, in_=c2, func=AF.Exp, scale=-0.5),
           inc={"S": 1})
    C.step(v, lambda e: e.tensor_scalar(out=XLN[:], in0=XT[:], scalar1=c0,
                                        scalar2=c2, op0=ALU.add, op1=ALU.mult),
           inc={"V": 1}, wait=C.w("S"))


def emit_t32(C, src, dstv, PT, PRM, n=32, w0=None):
    for jb in range((n + 3) // 4):
        k4 = min(4, n - 4 * jb)
        tw = C.w("V")
        for k in range(k4):
            j = 4 * jb + k
            C.step("t", lambda e, j=j, k=k: e.transpose(
                PT[:, k * 128:(k + 1) * 128], src[:, j * 128:(j + 1) * 128],
                PRM[:]), inc={"T": 1}, wait=tw)
        cw = {**C.w("T"), **(w0 or {})} if jb == 0 else C.w("T")
        C.step("v", lambda e, jb=jb, k4=k4: e.tensor_copy(
            out=dstv[:, 4 * jb:4 * jb + k4, :],
            in_=PT[:, 0:k4 * 128].rearrange("p (k c) -> p k c", k=k4)),
            inc={"V": 1}, wait=cw)


def build_qkv(layer):
    nc = bass.Bass("TRN2", target_bir_lowering=False)
    es = ExitStack()
    dd = {}

    def dram(name, shape, dtype=F32, kind="ExternalInput"):
        dd[name] = nc.dram_tensor(name, shape, dtype, kind=kind).ap()
        return dd[name]

    dram("x", [TPC, D])
    dram("wq", [D, D])
    dram("wk", [D, 1024])
    dram("wv", [D, 1024])
    dram("cos", [128, TPC])
    dram("sin", [128, TPC])
    dram("perm", [128, 128])
    dram("ident", [128, 128])
    dram("qfm", [D, TPC], kind="ExternalOutput")
    dram("kfm", [1024, TPC], kind="ExternalOutput")
    dram("vfm", [1024, TPC], kind="ExternalOutput")
    if layer == 0:
        dram("idxs", [D, 512], I32)
        dram("am2s", [64, 512])
        dram("sel2", [2, 128])
        dram("wdq", [D, 512], kind="ExternalOutput")

    XT = es.enter_context(nc.sbuf_tensor("XT", [128, D], F32))
    XLN = es.enter_context(nc.sbuf_tensor("XLN", [128, D], F32))
    XFM = es.enter_context(nc.sbuf_tensor("XFM", [128, 32 * 512], F32))
    WB = [es.enter_context(nc.sbuf_tensor(f"WB{i}", [128, 32 * 128], F32))
          for i in range(2)]
    CS = es.enter_context(nc.sbuf_tensor("CS", [128, 2 * TPC], F32))
    PRM = es.enter_context(nc.sbuf_tensor("PRM", [128, 128], F32))
    IDN = es.enter_context(nc.sbuf_tensor("IDN", [128, 128], F32))
    CB = es.enter_context(nc.sbuf_tensor("CB", [128, 8], F32))
    QB = es.enter_context(nc.sbuf_tensor("QB", [128, 512], F32))
    QC = es.enter_context(nc.sbuf_tensor("QC", [128, 512], F32))
    QD = es.enter_context(nc.sbuf_tensor("QD", [128, 512], F32))
    PT = es.enter_context(nc.psum_tensor("PT", [128, 512], F32))
    PP = [es.enter_context(nc.psum_tensor(f"PP{i}", [128, 512], F32))
          for i in range(2)]
    PR = es.enter_context(nc.psum_tensor("PR", [128, 512], F32))
    if layer == 0:
        IDT = es.enter_context(nc.sbuf_tensor("IDT", [128, 512], I32))
        AMR = es.enter_context(nc.sbuf_tensor("AMR", [2, 512], F32))
        SEL = es.enter_context(nc.sbuf_tensor("SEL", [2, 128], F32))
        XF = es.enter_context(nc.sbuf_tensor("XF", [128, 512], F32))
        VAL = es.enter_context(nc.sbuf_tensor("VAL", [128, 512], F32))
        TMP = es.enter_context(nc.sbuf_tensor("TMP", [128, 512], F32))
        PAM = es.enter_context(nc.psum_tensor("PAM", [128, 512], F32))

    C = Ctx(nc)
    C.sem("DX", "DW", "DO", "T", "V", "S")
    xfmv = XFM.ap().rearrange("p (j t) -> p j t", j=32)

    C.step("y", lambda e: e.dma_start(out=CS[:, 0:TPC], in_=dd["cos"]), inc={"DX": 16})
    C.step("y", lambda e: e.dma_start(out=CS[:, TPC:2 * TPC], in_=dd["sin"]), inc={"DX": 16})
    C.step("y", lambda e: e.dma_start(out=PRM[:], in_=dd["perm"]), inc={"DX": 16})
    C.step("y", lambda e: e.dma_start(out=IDN[:], in_=dd["ident"]), inc={"DX": 16})
    if layer == 0:
        C.step("y", lambda e: e.dma_start(out=SEL[:], in_=dd["sel2"]), inc={"DX": 16})

    def mk_rope(dst, tg, Tn, rope):
        def cb(oc, PSUM):
            tv = {**C.w("T"), "DO": C.c("DO")}
            C.step("v", lambda e: e.tensor_copy(out=QB[:, 0:Tn], in_=PSUM[:, 0:Tn]),
                   inc={"V": 1}, wait=tv)
            if rope:
                C.step("t", lambda e: e.matmul(PR[:, 0:Tn], PRM[:], QB[:, 0:Tn],
                                               start=True, stop=True),
                       inc={"T": 1}, wait=C.w("V"))
                C.step("v", lambda e: e.tensor_mul(
                    out=QC[:, 0:Tn], in0=QB[:, 0:Tn],
                    in1=CS[:, tg * 512:tg * 512 + Tn]), inc={"V": 1})
                C.step("v", lambda e: e.tensor_mul(
                    out=QD[:, 0:Tn], in0=PR[:, 0:Tn],
                    in1=CS[:, TPC + tg * 512:TPC + tg * 512 + Tn]),
                    inc={"V": 1}, wait=C.w("T"))
                C.step("v", lambda e: e.tensor_add(out=QB[:, 0:Tn], in0=QC[:, 0:Tn],
                                                   in1=QD[:, 0:Tn]), inc={"V": 1})
            C.step("y", lambda e, oc=oc: e.dma_start(
                out=dst[128 * oc:128 * (oc + 1), tg * 512:tg * 512 + Tn],
                in_=QB[:, 0:Tn]), inc={"DO": 16}, wait=C.w("V"))
        return cb

    for tg in range(2):
        for tl in range(4):
            tt = 4 * tg + tl
            C.step("y", lambda e, tt=tt: e.dma_start(
                out=XT[:], in_=dd["x"][128 * tt:128 * (tt + 1), :]),
                inc={"DX": 16}, wait=C.w("V", "S", "T"))
            emit_ln(C, XT, XLN, CB, {**C.w("DX"), "T": C.c("T")})
            emit_t32(C, XLN, xfmv[:, :, 128 * tl:128 * (tl + 1)], PT, IDN)
        def proj(wname, n_oc, dst, rope):
            cb = mk_rope(dst, tg, 512, rope)
            for oc in range(n_oc):
                buf = WB[oc % 2]
                C.step("g", lambda e, oc=oc, buf=buf, wname=wname: e.dma_start(
                    out=buf.ap().rearrange("p (j o) -> p j o", j=32),
                    in_=dd[wname][:, 128 * oc:128 * (oc + 1)]
                        .rearrange("(j p) o -> p j o", p=128)),
                    inc={"DW": 16}, wait=C.w("T"))
                ww = {**C.w("DW"), "V": C.c("V")}
                for j in range(32):
                    C.step("t", lambda e, j=j, buf=buf, oc=oc: e.matmul(
                        PP[oc % 2][:], buf.ap().rearrange("p (j o) -> p j o", j=32)[:, j, :],
                        xfmv[:, j, :], start=(j == 0), stop=(j == 31)),
                        inc={"T": 1} if j == 31 else None,
                        wait=(ww if j == 0 else None))
                cb(oc, PP[oc % 2])
        proj("wq", 32, dd["qfm"], True)
        proj("wk", 8, dd["kfm"], True)
        proj("wv", 8, dd["vfm"], False)

    if layer == 0:
        for j in range(32):
            C.step("y", lambda e, j=j: e.dma_start(
                out=IDT[:], in_=dd["idxs"][128 * j:128 * (j + 1), :]),
                inc={"DX": 16}, wait=C.w("V"))
            C.step("y", lambda e, j=j: e.dma_start(
                out=AMR[:], in_=dd["am2s"][2 * j:2 * j + 2, :]),
                inc={"DX": 16}, wait=C.w("T"))
            dw = C.w("DX")
            C.step("t", lambda e: e.matmul(PAM[:], SEL[:], AMR[:],
                                           start=True, stop=True),
                   inc={"T": 1}, wait={**dw, "V": C.c("V")})
            C.step("v", lambda e: e.tensor_copy(out=XF[:], in_=IDT[:]),
                   inc={"V": 1}, wait=dw)
            C.step("v", lambda e: e.tensor_scalar(out=VAL[:], in0=XF[:],
                                                  scalar1=0.0, scalar2=NF4[0],
                                                  op0=ALU.is_equal, op1=ALU.mult),
                   inc={"V": 1}, wait={"DO": C.c("DO")})
            for k in range(1, 16):
                if NF4[k] == 0.0:
                    continue
                C.step("v", lambda e, k=k: e.tensor_scalar(
                    out=TMP[:], in0=XF[:], scalar1=float(k), scalar2=NF4[k],
                    op0=ALU.is_equal, op1=ALU.mult), inc={"V": 1})
                C.step("v", lambda e: e.tensor_add(out=VAL[:], in0=VAL[:],
                                                   in1=TMP[:]), inc={"V": 1})
            C.step("v", lambda e: e.tensor_mul(out=VAL[:], in0=VAL[:], in1=PAM[:]),
                   inc={"V": 1}, wait=C.w("T"))
            C.step("y", lambda e, j=j: e.dma_start(
                out=dd["wdq"][128 * j:128 * (j + 1), :], in_=VAL[:]),
                inc={"DO": 16}, wait=C.w("V"))

    ndo = C.c("DO")
    C.step("y", lambda e: None, wait={"DO": ndo})
    finish(nc, C, es)
    return nc


def build_attn(topk):
    nc = bass.Bass("TRN2", target_bir_lowering=False)
    es = ExitStack()
    dd = {}

    def dram(name, shape, dtype=F32, kind="ExternalInput"):
        dd[name] = nc.dram_tensor(name, shape, dtype, kind=kind).ap()
        return dd[name]

    dram("qfm", [1024, S])
    dram("kfm", [256, S])
    dram("vtm", [S, 256])
    dram("diag", [128, 128])
    dram("ident", [128, 128])
    dram("aofm", [1024, S], kind="ExternalOutput")

    KF = es.enter_context(nc.sbuf_tensor("KF", [128, 2 * S], F32))
    VT = es.enter_context(nc.sbuf_tensor("VT", [128, 32 * 256], F32))
    QF = [es.enter_context(nc.sbuf_tensor(f"QF{i}", [128, S], F32))
          for i in range(2)]
    SC = es.enter_context(nc.sbuf_tensor("SC", [128, 4 * S], F32))
    MS = es.enter_context(nc.sbuf_tensor("MS", [128, S], F32))
    ET = es.enter_context(nc.sbuf_tensor("ET", [128, 512], F32))
    OB = es.enter_context(nc.sbuf_tensor("OB", [128, 128], F32))
    DIA = es.enter_context(nc.sbuf_tensor("DIA", [128, 128], F32))
    IDN = es.enter_context(nc.sbuf_tensor("IDN", [128, 128], F32))
    CL = es.enter_context(nc.sbuf_tensor("CL", [128, 48], F32))
    SP = [es.enter_context(nc.psum_tensor(f"SP{i}", [128, 512], F32))
          for i in range(2)]
    PT = es.enter_context(nc.psum_tensor("PT", [128, 512], F32))
    AV = [es.enter_context(nc.psum_tensor(f"AV{i}", [128, 128], F32))
          for i in range(2)]

    kfv = KF.ap().rearrange("p (s t) -> p s t", s=2)
    vtv = VT.ap().rearrange("p (c d) -> p c d", c=32)
    scv = SC.ap().rearrange("p (l t) -> p l t", l=4)
    LO, W4 = CL[:, 0:4], CL[:, 4:8]
    MID, CN = CL[:, 8:12], CL[:, 12:16]
    GE, GW = CL[:, 16:20], CL[:, 20:24]
    RX, NRX = CL[:, 24:28], CL[:, 28:32]
    ZZ, ZR = CL[:, 32:36], CL[:, 36:40]

    C = Ctx(nc)
    C.sem("DX", "DW", "DO", "T", "V", "S")

    C.step("y", lambda e: e.dma_start(
        out=kfv, in_=dd["kfm"].rearrange("(s p) t -> p s t", p=128)), inc={"DX": 16})
    C.step("y", lambda e: e.dma_start(
        out=vtv, in_=dd["vtm"].rearrange("(c p) d -> p c d", p=128)), inc={"DX": 16})
    C.step("y", lambda e: e.dma_start(out=DIA[:], in_=dd["diag"]), inc={"DX": 16})
    C.step("y", lambda e: e.dma_start(out=IDN[:], in_=dd["ident"]), inc={"DX": 16})
    C.step("g", lambda e: e.dma_start(out=QF[0][:], in_=dd["qfm"][0:128, :]),
           inc={"DW": 16})

    for h in range(8):
        kv = h // 4
        qfw = {"DW": 16 * (h + 1), "DX": 16 * 4}
        if h + 1 < 8:
            C.step("g", lambda e, h=h: e.dma_start(
                out=QF[(h + 1) % 2][:], in_=dd["qfm"][128 * (h + 1):128 * (h + 2), :]),
                inc={"DW": 16}, wait=C.w("T"))
        qf = QF[h % 2]
        for g in range(8):
            for tl in range(4):
                t = 4 * g + tl
                L = 512 * g + 128 * (tl + 1)
                for kc in range(g + 1):
                    wd = 512 if kc < g else 128 * (tl + 1)
                    C.step("t", lambda e, kc=kc, wd=wd, t=t, kv=kv, qf=qf: e.matmul(
                        SP[kc % 2][:, 0:wd], qf[:, 128 * t:128 * (t + 1)],
                        kfv[:, kv, 512 * kc:512 * kc + wd], start=True, stop=True),
                        inc={"T": 1}, wait={**qfw, "V": C.c("V")})
                    C.step("v", lambda e, kc=kc, wd=wd, tl=tl: e.tensor_copy(
                        out=scv[:, tl, 512 * kc:512 * kc + wd],
                        in_=SP[kc % 2][:, 0:wd]), inc={"V": 1}, wait=C.w("T"))
                if topk:
                    C.step("v", lambda e, tl=tl, L=L: e.tensor_reduce(
                        out=LO[:, tl:tl + 1], in_=scv[:, tl, 0:L], axis=AX.X,
                        op=ALU.min), inc={"V": 1})
                C.step("v", lambda e, tl=tl, L=L: e.tensor_add(
                    out=scv[:, tl, L - 128:L], in0=scv[:, tl, L - 128:L],
                    in1=DIA[:]), inc={"V": 1})
                C.step("v", lambda e, tl=tl, L=L: e.tensor_reduce(
                    out=RX[:, tl:tl + 1], in_=scv[:, tl, 0:L], axis=AX.X,
                    op=ALU.max), inc={"V": 1})
            if topk:
                C.step("v", lambda e: e.scalar_tensor_tensor(
                    out=W4, in0=RX, scalar=1.0, in1=LO, op0=ALU.add,
                    op1=ALU.subtract), inc={"V": 1})
                for it in range(NBIS):
                    C.step("v", lambda e: e.tensor_scalar(
                        out=W4, in0=W4, scalar1=0.5, scalar2=None,
                        op0=ALU.mult), inc={"V": 1})
                    C.step("v", lambda e: e.tensor_add(out=MID, in0=LO, in1=W4),
                           inc={"V": 1})
                    for tl in range(4):
                        L = 512 * g + 128 * (tl + 1)
                        C.step("v", lambda e, tl=tl, L=L: e.tensor_scalar(
                            out=MS[:, 0:L], in0=scv[:, tl, 0:L],
                            scalar1=MID[:, tl:tl + 1], scalar2=0.0,
                            op0=ALU.is_ge, op1=ALU.add,
                            accum_out=CN[:, tl:tl + 1]), inc={"V": 1})
                    C.step("v", lambda e: e.tensor_scalar(
                        out=GE, in0=CN, scalar1=KKEEP, scalar2=None,
                        op0=ALU.is_ge), inc={"V": 1})
                    C.step("v", lambda e: e.tensor_mul(out=GW, in0=W4, in1=GE),
                           inc={"V": 1})
                    C.step("v", lambda e: e.tensor_add(out=LO, in0=LO, in1=GW),
                           inc={"V": 1})
            C.step("v", lambda e: e.tensor_scalar(
                out=NRX, in0=RX, scalar1=-RSQ, scalar2=None, op0=ALU.mult),
                inc={"V": 1})
            for tl in range(4):
                t = 4 * g + tl
                L = 512 * g + 128 * (tl + 1)
                if topk:
                    C.step("v", lambda e, tl=tl, L=L: e.tensor_scalar(
                        out=MS[:, 0:L], in0=scv[:, tl, 0:L],
                        scalar1=LO[:, tl:tl + 1], scalar2=None,
                        op0=ALU.is_ge), inc={"V": 1})
                    C.step("s", lambda e, tl=tl, L=L: e.activation(
                        out=scv[:, tl, 0:L], in_=scv[:, tl, 0:L], func=AF.Exp,
                        scale=RSQ, bias=NRX[:, tl:tl + 1]), inc={"S": 1},
                        wait=C.w("V"))
                    C.step("v", lambda e, tl=tl, L=L: e.scalar_tensor_tensor(
                        out=scv[:, tl, 0:L], in0=scv[:, tl, 0:L], scalar=1.0,
                        in1=MS[:, 0:L], op0=ALU.mult, op1=ALU.mult,
                        accum_out=ZZ[:, tl:tl + 1]), inc={"V": 1}, wait=C.w("S"))
                else:
                    C.step("s", lambda e, tl=tl, L=L: e.activation(
                        out=scv[:, tl, 0:L], in_=scv[:, tl, 0:L], func=AF.Exp,
                        scale=RSQ, bias=NRX[:, tl:tl + 1],
                        accum_out=ZZ[:, tl:tl + 1]), inc={"S": 1}, wait=C.w("V"))
                C.step("v", lambda e, tl=tl: e.reciprocal(
                    out=ZR[:, tl:tl + 1], in_=ZZ[:, tl:tl + 1]), inc={"V": 1},
                    wait=C.w("S"))
                C.step("v", lambda e, tl=tl, L=L: e.tensor_scalar(
                    out=scv[:, tl, 0:L], in0=scv[:, tl, 0:L],
                    scalar1=ZR[:, tl:tl + 1], scalar2=None, op0=ALU.mult),
                    inc={"V": 1})
                nch = t + 1
                ngrp = (nch + 3) // 4
                for grp in range(ngrp):
                    k4 = min(4, nch - 4 * grp)
                    tw = C.w("V")
                    for k in range(k4):
                        C.step("t", lambda e, k=k, grp=grp, tl=tl: e.transpose(
                            PT[:, 128 * k:128 * (k + 1)],
                            scv[:, tl, (4 * grp + k) * 128:(4 * grp + k + 1) * 128],
                            IDN[:]), inc={"T": 1}, wait=tw)
                    C.step("v", lambda e, k4=k4: e.tensor_copy(
                        out=ET[:, 0:128 * k4], in_=PT[:, 0:128 * k4]),
                        inc={"V": 1}, wait=C.w("T"))
                    for k in range(k4):
                        C.step("t", lambda e, k=k, grp=grp, kv=kv, tl=tl,
                               last=(grp == ngrp - 1 and k == k4 - 1): e.matmul(
                            AV[tl % 2][:], vtv[:, 4 * grp + k, 128 * kv:128 * (kv + 1)],
                            ET[:, 128 * k:128 * (k + 1)],
                            start=(grp == 0 and k == 0), stop=last),
                            inc={"T": 1}, wait=C.w("V"))
                C.step("v", lambda e, tl=tl: e.tensor_copy(out=OB[:], in_=AV[tl % 2][:]),
                       inc={"V": 1}, wait={**C.w("T"), "DO": C.c("DO")})
                C.step("y", lambda e, h=h, t=t: e.dma_start(
                    out=dd["aofm"][128 * h:128 * (h + 1), 128 * t:128 * (t + 1)],
                    in_=OB[:]), inc={"DO": 16}, wait=C.w("V"))

    ndo = C.c("DO")
    C.step("y", lambda e: None, wait={"DO": ndo})
    finish(nc, C, es)
    return nc


def build_wo(with_ffn):
    nc = bass.Bass("TRN2", target_bir_lowering=False)
    es = ExitStack()
    dd = {}

    def dram(name, shape, dtype=F32, kind="ExternalInput"):
        dd[name] = nc.dram_tensor(name, shape, dtype, kind=kind).ap()
        return dd[name]

    dram("xin", [TPC, D])
    dram("aofm", [D, TPC])
    dram("wo", [D, D])
    dram("ident", [128, 128])
    dram("xout", [TPC, D], kind="ExternalOutput")
    if with_ffn:
        dram("wdq", [D, D])
        dram("fbias", [1, D])
    else:
        dram("vmask", [128, 16])
        dram("h1fm", [D, TPC], kind="ExternalOutput")
        dram("parts", [2, 1], kind="ExternalOutput")

    AOT = es.enter_context(nc.sbuf_tensor("AOT", [128, 32 * 128], F32))
    XT = es.enter_context(nc.sbuf_tensor("XT", [128, D], F32))
    X1 = es.enter_context(nc.sbuf_tensor("X1", [128, D], F32))
    HB = es.enter_context(nc.sbuf_tensor("HB", [128, D], F32))
    HFMT = es.enter_context(nc.sbuf_tensor("HFMT", [128, 32 * 128], F32))
    WS = [es.enter_context(nc.sbuf_tensor(f"WS{i}", [128, 512], F32))
          for i in range(2)]
    IDN = es.enter_context(nc.sbuf_tensor("IDN", [128, 128], F32))
    CB = es.enter_context(nc.sbuf_tensor("CB", [128, 8], F32))
    BIA = es.enter_context(nc.sbuf_tensor("BIA", [128, D], F32))
    ONE1 = es.enter_context(nc.sbuf_tensor("ONE1", [1, 128], F32))
    BR = es.enter_context(nc.sbuf_tensor("BR", [1, D], F32))
    VM = es.enter_context(nc.sbuf_tensor("VM", [128, 16], F32))
    PQ = [es.enter_context(nc.psum_tensor(f"PQ{i}", [128, 512], F32))
          for i in range(2)]
    PT = es.enter_context(nc.psum_tensor("PT", [128, 512], F32))
    PB = es.enter_context(nc.psum_tensor("PB", [128, 512], F32))
    PC2 = es.enter_context(nc.psum_tensor("PC2", [2, 1], F32))

    aotv = AOT.ap().rearrange("p (j t) -> p j t", j=32)
    hfv = HFMT.ap().rearrange("p (j t) -> p j t", j=32)

    C = Ctx(nc)
    C.sem("DX", "DW", "DO", "T", "V", "S")
    C.step("y", lambda e: e.dma_start(out=IDN[:], in_=dd["ident"]), inc={"DX": 16})
    if with_ffn:
        C.step("y", lambda e: e.dma_start(out=BR[:], in_=dd["fbias"]), inc={"DX": 16})
        C.step("v", lambda e: e.memset(ONE1[:], 1.0), inc={"V": 1})
        for oc in range(8):
            C.step("t", lambda e, oc=oc: e.matmul(
                PB[:], ONE1[:], BR[0:1, 512 * oc:512 * (oc + 1)],
                start=True, stop=True), inc={"T": 1},
                wait={**C.w("DX"), "V": C.c("V")})
            C.step("v", lambda e, oc=oc: e.tensor_copy(
                out=BIA[:, 512 * oc:512 * (oc + 1)], in_=PB[:]),
                inc={"V": 1}, wait=C.w("T"))
    else:
        C.step("y", lambda e: e.dma_start(out=VM[:], in_=dd["vmask"]), inc={"DX": 16})

    lastuse = {0: 0, 1: 0}

    def stream_mm(wname, lhsv, extra0):
        for oc in range(8):
            for j in range(32):
                sl = (oc * 32 + j) % 2
                C.step("g", lambda e, j=j, oc=oc, sl=sl, wname=wname: e.dma_start(
                    out=WS[sl][:], in_=dd[wname][128 * j:128 * (j + 1),
                                                 512 * oc:512 * (oc + 1)]),
                    inc={"DW": 16}, wait={"T": lastuse[sl]})
                ww = {"DW": C.c("DW"), "V": C.c("V"),
                      **(extra0 if (oc == 0 and j == 0) else {})}
                C.step("t", lambda e, j=j, oc=oc, sl=sl: e.matmul(
                    PQ[oc % 2][:], lhsv[:, j, :], WS[sl][:],
                    start=(j == 0), stop=(j == 31)), inc={"T": 1}, wait=ww)
                lastuse[sl] = C.c("T")
            yield oc

    for tt in range(8):
        C.step("y", lambda e, tt=tt: e.dma_start(
            out=aotv, in_=dd["aofm"][:, 128 * tt:128 * (tt + 1)]
                .rearrange("(j p) t -> p j t", p=128)),
            inc={"DX": 16}, wait=C.w("T", "V"))
        C.step("y", lambda e, tt=tt: e.dma_start(
            out=XT[:], in_=dd["xin"][128 * tt:128 * (tt + 1), :]),
            inc={"DX": 16}, wait=C.w("V", "S"))
        ldw = {**C.w("DX"), "S": C.c("S")}
        for oc in stream_mm("wo", aotv, ldw):
            C.step("v", lambda e, oc=oc: e.tensor_add(
                out=X1[:, 512 * oc:512 * (oc + 1)], in0=PQ[oc % 2][:],
                in1=XT[:, 512 * oc:512 * (oc + 1)]), inc={"V": 1},
                wait={**C.w("T"), "S": C.c("S"), "DO": C.c("DO")})
        if not with_ffn:
            C.step("y", lambda e, tt=tt: e.dma_start(
                out=dd["xout"][128 * tt:128 * (tt + 1), :], in_=X1[:]),
                inc={"DO": 16}, wait=C.w("V"))
        emit_ln(C, X1, HB, CB, {**C.w("V"), "DO": C.c("DO"), "T": C.c("T")})
        emit_t32(C, HB, hfv, PT, IDN, w0={"DO": C.c("DO")})
        if with_ffn:
            for oc in stream_mm("wdq", hfv, {}):
                C.step("v", lambda e, oc=oc: e.tensor_add(
                    out=HB[:, 512 * oc:512 * (oc + 1)], in0=PQ[oc % 2][:],
                    in1=BIA[:, 512 * oc:512 * (oc + 1)]), inc={"V": 1},
                    wait=C.w("T"))
                C.step("v", lambda e, oc=oc: e.tensor_add(
                    out=HB[:, 512 * oc:512 * (oc + 1)],
                    in0=HB[:, 512 * oc:512 * (oc + 1)],
                    in1=X1[:, 512 * oc:512 * (oc + 1)]), inc={"V": 1})
            C.step("y", lambda e, tt=tt: e.dma_start(
                out=dd["xout"][128 * tt:128 * (tt + 1), :], in_=HB[:]),
                inc={"DO": 16}, wait=C.w("V"))
        else:
            C.step("y", lambda e, tt=tt: e.dma_start(
                out=dd["h1fm"][:, 128 * tt:128 * (tt + 1)]
                    .rearrange("(j p) t -> p j t", p=128), in_=hfv),
                inc={"DO": 16}, wait=C.w("V"))
            C.step("v", lambda e: e.tensor_reduce(
                out=CB[:, 4:5], in_=HB[:], axis=AX.X, op=ALU.add,
                apply_absolute_value=True), inc={"V": 1}, wait=C.w("T"))
            C.step("t", lambda e, tt=tt: e.matmul(
                PC2[:], VM[:, 2 * tt:2 * tt + 2], CB[:, 4:5],
                start=(tt == 0), stop=(tt == 7)), inc={"T": 1}, wait=C.w("V"))
    if not with_ffn:
        C.step("v", lambda e: e.tensor_copy(out=CB[0:2, 6:7], in_=PC2[:]),
               inc={"V": 1}, wait=C.w("T"))
        C.step("y", lambda e: e.dma_start(out=dd["parts"], in_=CB[0:2, 6:7]),
               inc={"DO": 16}, wait=C.w("V"))
    ndo = C.c("DO")
    C.step("y", lambda e: None, wait={"DO": ndo})
    finish(nc, C, es)
    return nc


def build_moe():
    nc = bass.Bass("TRN2", target_bir_lowering=False)
    es = ExitStack()
    dd = {}

    def dram(name, shape, dtype=F32, kind="ExternalInput"):
        dd[name] = nc.dram_tensor(name, shape, dtype, kind=kind).ap()
        return dd[name]

    dram("h1fm", [D, TPC])
    dram("x3", [TPC, D])
    dram("parts4", [1, 8])
    dram("h0m", [128, 32])
    dram("gwm", [128, 96])
    dram("gtail", [1, 8])
    dram("w1", [3, D, D])
    dram("b1c", [3, 128, 32])
    dram("w2", [3, D, D])
    dram("b2r", [3, D])
    dram("x4", [TPC, D], kind="ExternalOutput")

    ACTA = es.enter_context(nc.sbuf_tensor("ACTA", [128, 32 * 512], F32))
    OUTA = es.enter_context(nc.sbuf_tensor("OUTA", [128, 4 * 4096], F32))
    H1J = [es.enter_context(nc.sbuf_tensor(f"H1J{i}", [128, 512], F32))
           for i in range(2)]
    W1B = [es.enter_context(nc.sbuf_tensor(f"W1B{i}", [128, 128], F32))
           for i in range(2)]
    W2B = [es.enter_context(nc.sbuf_tensor(f"W2B{i}", [128, 512], F32))
           for i in range(2)]
    H1T = [es.enter_context(nc.sbuf_tensor(f"H1T{i}", [128, 128], F32))
           for i in range(2)]
    GW = es.enter_context(nc.sbuf_tensor("GW", [128, 96], F32))
    H0 = es.enter_context(nc.sbuf_tensor("H0", [128, 32], F32))
    B1C = es.enter_context(nc.sbuf_tensor("B1C", [128, 32], F32))
    B2R = es.enter_context(nc.sbuf_tensor("B2R", [1, 2048], F32))
    B2A = es.enter_context(nc.sbuf_tensor("B2A", [128, 2048], F32))
    IN1 = es.enter_context(nc.sbuf_tensor("IN1", [128, 3], F32))
    IN2 = es.enter_context(nc.sbuf_tensor("IN2", [128, 3], F32))
    TAB = es.enter_context(nc.sbuf_tensor("TAB", [128, 3], F32))
    LG = es.enter_context(nc.sbuf_tensor("LG", [128, 3], F32))
    T3 = es.enter_context(nc.sbuf_tensor("T3", [128, 3], F32))
    GCA = es.enter_context(nc.sbuf_tensor("GCA", [128, 12], F32))
    CL = es.enter_context(nc.sbuf_tensor("CL", [128, 16], F32))
    ROWS = es.enter_context(nc.sbuf_tensor("ROWS", [1, 64], F32))
    ONE1 = es.enter_context(nc.sbuf_tensor("ONE1", [1, 128], F32))
    XT3 = es.enter_context(nc.sbuf_tensor("XT3", [128, 4096], F32))
    TMPE = es.enter_context(nc.sbuf_tensor("TMPE", [128, 512], F32))
    PRE = [es.enter_context(nc.psum_tensor(f"PRE{i}", [128, 512], F32))
           for i in range(2)]
    PY = [es.enter_context(nc.psum_tensor(f"PY{i}", [128, 512], F32))
          for i in range(4)]
    PG2 = es.enter_context(nc.psum_tensor("PG2", [128, 8], F32))
    PB = es.enter_context(nc.psum_tensor("PB", [128, 512], F32))

    gwv = GW.ap().rearrange("p (j o) -> p j o", j=32)
    actv = ACTA.ap().rearrange("p (j t) -> p j t", j=32)
    outv = OUTA.ap().rearrange("p (l o) -> p l o", l=4)
    FN = [AF.Tanh, AF.Gelu, AF.Silu]
    VL = int(S * 0.3)

    C = Ctx(nc)
    C.sem("DX", "DW", "DY", "DO", "T", "V", "S")
    lastT = {}

    def ld(eng, buf, src, key):
        sem = "DW" if eng == "g" else "DY"
        C.step(eng, lambda e: e.dma_start(out=buf, in_=src),
               inc={sem: 16}, wait={"T": lastT.get(key, 0)})
        return (sem, C.c(sem))

    for buf, src in [(ROWS[0:1, 0:8], dd["parts4"]), (ROWS[0:1, 8:16], dd["gtail"]),
                     (H0[:], dd["h0m"]), (GW[:], dd["gwm"])]:
        C.step("y", lambda e, buf=buf, src=src: e.dma_start(out=buf, in_=src),
               inc={"DX": 16})
    C.step("v", lambda e: e.memset(ONE1[:], 1.0), inc={"V": 1})
    ldw0 = C.w("DX")
    C.step("v", lambda e: e.tensor_reduce(
        out=ROWS[0:1, 16:18], in_=ROWS.ap()[0:1, 0:8].rearrange("o (j k) -> o k j", j=4),
        axis=AX.X, op=ALU.add), inc={"V": 1}, wait=ldw0)
    C.step("v", lambda e: e.tensor_scalar(
        out=ROWS[0:1, 18:19], in0=ROWS[0:1, 16:17],
        scalar1=1.0 / (VL * D), scalar2=1e-8, op0=ALU.mult, op1=ALU.add),
        inc={"V": 1})
    C.step("v", lambda e: e.tensor_scalar(
        out=ROWS[0:1, 19:20], in0=ROWS[0:1, 17:18],
        scalar1=1.0 / ((S - VL) * D), scalar2=None, op0=ALU.mult), inc={"V": 1})
    C.step("v", lambda e: e.reciprocal(out=ROWS[0:1, 20:21], in_=ROWS[0:1, 18:19]),
           inc={"V": 1})
    C.step("v", lambda e: e.tensor_mul(out=ROWS[0:1, 21:22], in0=ROWS[0:1, 19:20],
                                       in1=ROWS[0:1, 20:21]), inc={"V": 1})
    C.step("v", lambda e: e.tensor_scalar(
        out=ROWS[0:1, 40:43], in0=ROWS[0:1, 8:11], scalar1=ROWS[0:1, 21:22],
        scalar2=None, op0=ALU.mult), inc={"V": 1})
    C.step("v", lambda e: e.tensor_add(out=ROWS[0:1, 40:43], in0=ROWS[0:1, 40:43],
                                       in1=ROWS[0:1, 11:14]), inc={"V": 1})
    for j in range(32):
        C.step("t", lambda e, j=j: e.matmul(
            PG2[0:1, 0:3], H0[:, j:j + 1], gwv[:, j, :],
            start=(j == 0), stop=(j == 31)),
            inc={"T": 1} if j == 31 else None,
            wait=(ldw0 if j == 0 else None))
    C.step("v", lambda e: e.tensor_copy(out=ROWS[0:1, 24:27], in_=PG2[0:1, 0:3]),
           inc={"V": 1}, wait=C.w("T"))
    C.step("v", lambda e: e.tensor_add(out=ROWS[0:1, 28:31], in0=ROWS[0:1, 40:43],
                                       in1=ROWS[0:1, 24:27]), inc={"V": 1})
    C.step("v", lambda e: e.tensor_reduce(out=ROWS[0:1, 31:32], in_=ROWS[0:1, 28:31],
                                          axis=AX.X, op=ALU.max), inc={"V": 1})
    C.step("v", lambda e: e.tensor_reduce(out=ROWS[0:1, 32:33], in_=ROWS[0:1, 28:31],
                                          axis=AX.X, op=ALU.min), inc={"V": 1})
    C.step("v", lambda e: e.tensor_scalar(
        out=ROWS[0:1, 33:36], in0=ROWS[0:1, 28:31], scalar1=ROWS[0:1, 31:32],
        scalar2=None, op0=ALU.is_equal), inc={"V": 1})
    C.step("v", lambda e: e.tensor_scalar(
        out=ROWS[0:1, 36:39], in0=ROWS[0:1, 28:31], scalar1=ROWS[0:1, 32:33],
        scalar2=None, op0=ALU.is_equal), inc={"V": 1})
    C.step("v", lambda e: e.tensor_scalar(
        out=ROWS[0:1, 44:47], in0=ROWS[0:1, 33:36], scalar1=-1.0, scalar2=1.0,
        op0=ALU.mult, op1=ALU.add), inc={"V": 1})
    C.step("v", lambda e: e.tensor_scalar(
        out=ROWS[0:1, 47:50], in0=ROWS[0:1, 36:39], scalar1=-1.0, scalar2=1.0,
        op0=ALU.mult, op1=ALU.add), inc={"V": 1})
    C.step("v", lambda e: e.tensor_mul(out=ROWS[0:1, 47:50], in0=ROWS[0:1, 44:47],
                                       in1=ROWS[0:1, 47:50]), inc={"V": 1})
    for dst, cols in [(IN1, (33, 36)), (IN2, (47, 50)), (TAB, (40, 43))]:
        C.step("t", lambda e, cols=cols: e.matmul(
            PB[:, 0:3], ONE1[:], ROWS[0:1, cols[0]:cols[1]],
            start=True, stop=True), inc={"T": 1}, wait=C.w("V"))
        C.step("v", lambda e, dst=dst: e.tensor_copy(out=dst[:], in_=PB[:, 0:3]),
               inc={"V": 1}, wait=C.w("T"))

    for th in range(2):
        for t4 in range(4):
            for j in range(32):
                sl = j % 2
                ks, k = ld("g", H1T[sl][:], dd["h1fm"][128 * j:128 * (j + 1),
                                                   512 * th + 128 * t4:512 * th + 128 * (t4 + 1)],
                       ("h1t", sl))
                C.step("t", lambda e, j=j, sl=sl: e.matmul(
                    PG2[:, 0:3], H1T[sl][:], gwv[:, j, :],
                    start=(j == 0), stop=(j == 31)),
                    inc={"T": 1}, wait={ks: k, "V": C.c("V")})
                lastT[("h1t", sl)] = C.c("T")
            C.step("v", lambda e: e.tensor_add(out=LG[:], in0=PG2[:, 0:3], in1=TAB[:]),
                   inc={"V": 1}, wait=C.w("T"))
            C.step("v", lambda e: e.tensor_reduce(out=CL[:, 0:1], in_=LG[:],
                                                  axis=AX.X, op=ALU.max), inc={"V": 1})
            C.step("v", lambda e: e.tensor_scalar(out=CL[:, 1:2], in0=CL[:, 0:1],
                                                  scalar1=-1.0, scalar2=None,
                                                  op0=ALU.mult), inc={"V": 1})
            C.step("s", lambda e: e.activation(out=LG[:], in_=LG[:], func=AF.Exp,
                                               bias=CL[:, 1:2]), inc={"S": 1},
                   wait=C.w("V"))
            C.step("v", lambda e: e.tensor_reduce(out=CL[:, 2:3], in_=LG[:],
                                                  axis=AX.X, op=ALU.add),
                   inc={"V": 1}, wait=C.w("S"))
            C.step("v", lambda e: e.reciprocal(out=CL[:, 3:4], in_=CL[:, 2:3]),
                   inc={"V": 1})
            C.step("v", lambda e: e.tensor_scalar(out=LG[:], in0=LG[:],
                                                  scalar1=CL[:, 3:4], scalar2=None,
                                                  op0=ALU.mult), inc={"V": 1})
            C.step("v", lambda e: e.tensor_reduce(out=CL[:, 4:5], in_=LG[:],
                                                  axis=AX.X, op=ALU.max), inc={"V": 1})
            C.step("v", lambda e: e.tensor_reduce(out=CL[:, 5:6], in_=LG[:],
                                                  axis=AX.X, op=ALU.min), inc={"V": 1})
            C.step("v", lambda e: e.tensor_reduce(out=CL[:, 6:7], in_=LG[:],
                                                  axis=AX.X, op=ALU.add), inc={"V": 1})
            C.step("v", lambda e: e.scalar_tensor_tensor(
                out=CL[:, 7:8], in0=CL[:, 6:7], scalar=CL[:, 4:5], in1=CL[:, 5:6],
                op0=ALU.subtract, op1=ALU.subtract), inc={"V": 1})
            C.step("v", lambda e: e.scalar_tensor_tensor(
                out=CL[:, 8:9], in0=CL[:, 4:5], scalar=1e-8, in1=CL[:, 7:8],
                op0=ALU.add, op1=ALU.add), inc={"V": 1})
            C.step("v", lambda e: e.reciprocal(out=CL[:, 9:10], in_=CL[:, 8:9]),
                   inc={"V": 1})
            C.step("v", lambda e: e.tensor_mul(out=CL[:, 10:11], in0=CL[:, 4:5],
                                               in1=CL[:, 9:10]), inc={"V": 1})
            C.step("v", lambda e: e.tensor_mul(out=CL[:, 11:12], in0=CL[:, 7:8],
                                               in1=CL[:, 9:10]), inc={"V": 1})
            C.step("v", lambda e, t4=t4: e.tensor_scalar(
                out=GCA[:, 3 * t4:3 * t4 + 3], in0=IN1[:], scalar1=CL[:, 10:11],
                scalar2=None, op0=ALU.mult), inc={"V": 1})
            C.step("v", lambda e: e.tensor_scalar(
                out=T3[:], in0=IN2[:], scalar1=CL[:, 11:12], scalar2=None,
                op0=ALU.mult), inc={"V": 1})
            C.step("v", lambda e, t4=t4: e.tensor_add(
                out=GCA[:, 3 * t4:3 * t4 + 3], in0=GCA[:, 3 * t4:3 * t4 + 3],
                in1=T3[:]), inc={"V": 1})
        for ei in range(3):
            C.step("y", lambda e, ei=ei: e.dma_start(out=B1C[:], in_=dd["b1c"][ei]),
                   inc={"DX": 16}, wait=C.w("S"))
            b1w = C.w("DX")
            for fc in range(32):
                for j in range(32):
                    sl = j % 2
                    s1, k1 = ld("g", W1B[sl][:], dd["w1"][ei, 128 * j:128 * (j + 1),
                                                      128 * fc:128 * (fc + 1)],
                            ("w1", sl))
                    s2, k2 = ld("y", H1J[sl][:], dd["h1fm"][128 * j:128 * (j + 1),
                                                        512 * th:512 * (th + 1)],
                            ("h1j", sl))
                    C.step("t", lambda e, j=j, sl=sl, fc=fc: e.matmul(
                        PRE[fc % 2][:], W1B[sl][:], H1J[sl][:],
                        start=(j == 0), stop=(j == 31)),
                        inc={"T": 1},
                        wait={s1: k1, s2: k2, "S": C.c("S"), "V": C.c("V")})
                    lastT[("w1", sl)] = C.c("T")
                    lastT[("h1j", sl)] = C.c("T")
                C.step("s", lambda e, fc=fc, ei=ei: e.activation(
                    out=actv[:, fc, :], in_=PRE[fc % 2][:], func=FN[ei],
                    bias=B1C[:, fc:fc + 1]), inc={"S": 1},
                    wait={**C.w("T"), **b1w})
            for oh in range(2):
                C.step("y", lambda e, ei=ei, oh=oh: e.dma_start(
                    out=B2R[:], in_=dd["b2r"][ei:ei + 1, 2048 * oh:2048 * (oh + 1)]),
                    inc={"DX": 16}, wait=C.w("T"))
                b2w = C.w("DX")
                for q in range(4):
                    C.step("t", lambda e, q=q: e.matmul(
                        PB[:], ONE1[:], B2R[0:1, 512 * q:512 * (q + 1)],
                        start=True, stop=True), inc={"T": 1},
                        wait={**b2w, "V": C.c("V")})
                    C.step("v", lambda e, q=q: e.tensor_copy(
                        out=B2A[:, 512 * q:512 * (q + 1)], in_=PB[:]),
                        inc={"V": 1}, wait=C.w("T"))
                for oc in range(4):
                    for f in range(32):
                        sl = f % 2
                        s3, k3 = ld("g", W2B[sl][:],
                                dd["w2"][ei, 128 * f:128 * (f + 1),
                                         2048 * oh + 512 * oc:2048 * oh + 512 * (oc + 1)],
                                ("w2", sl))
                        for t4 in range(4):
                            C.step("t", lambda e, f=f, sl=sl, t4=t4: e.matmul(
                                PY[t4][:], actv[:, f, 128 * t4:128 * (t4 + 1)],
                                W2B[sl][:], start=(f == 0), stop=(f == 31)),
                                inc={"T": 1},
                                wait={s3: k3, "S": C.c("S"), "V": C.c("V")})
                        lastT[("w2", sl)] = C.c("T")
                    for t4 in range(4):
                        C.step("v", lambda e, t4=t4, oc=oc: e.tensor_add(
                            out=TMPE[:], in0=PY[t4][:],
                            in1=B2A[:, 512 * oc:512 * (oc + 1)]), inc={"V": 1},
                            wait=C.w("T"))
                        col = GCA[:, 0:12][:, 3 * t4 + ei:3 * t4 + ei + 1]
                        osl = outv[:, t4, 2048 * oh + 512 * oc:2048 * oh + 512 * (oc + 1)]
                        if ei == 0:
                            C.step("v", lambda e, osl=osl, col=col: e.tensor_scalar(
                                out=osl, in0=TMPE[:], scalar1=col, scalar2=None,
                                op0=ALU.mult), inc={"V": 1},
                                wait={"DO": C.c("DO")})
                        else:
                            C.step("v", lambda e, col=col: e.tensor_scalar(
                                out=TMPE[:], in0=TMPE[:], scalar1=col, scalar2=None,
                                op0=ALU.mult), inc={"V": 1})
                            C.step("v", lambda e, osl=osl: e.tensor_add(
                                out=osl, in0=osl, in1=TMPE[:]), inc={"V": 1})
        for t4 in range(4):
            tt = 4 * th + t4
            C.step("y", lambda e, tt=tt: e.dma_start(
                out=XT3[:], in_=dd["x3"][128 * tt:128 * (tt + 1), :]),
                inc={"DX": 16}, wait=C.w("V"))
            C.step("v", lambda e, t4=t4: e.tensor_add(
                out=outv[:, t4, :], in0=outv[:, t4, :], in1=XT3[:]),
                inc={"V": 1}, wait=C.w("DX"))
            C.step("y", lambda e, tt=tt, t4=t4: e.dma_start(
                out=dd["x4"][128 * tt:128 * (tt + 1), :], in_=outv[:, t4, :]),
                inc={"DO": 16}, wait=C.w("V"))
    ndo = C.c("DO")
    C.step("y", lambda e: None, wait={"DO": ndo})
    finish(nc, C, es)
    return nc


def _run(nc, maps):
    res = run_bass_kernel_spmd(nc, maps, core_ids=list(range(8)))
    return res.results


def _ca(x):
    return np.ascontiguousarray(x)


def kernel(L0_attn_hidden_states, L0_attn_Wq, L0_attn_Wk, L0_attn_Wv,
           L0_attn_Wo, L0_ffn_w_idx, L0_ffn_absmax, L0_ffn_bias,
           L1_attn_wq, L1_attn_wk, L1_attn_wv, L1_attn_wo,
           L1_ffn_gate_w, L1_ffn_gate_b, L1_ffn_w1, L1_ffn_b1,
           L1_ffn_w2, L1_ffn_b2):
    f32 = np.float32
    X0 = np.asarray(L0_attn_hidden_states, f32)
    diag = _diagneg()

    def shard(X, c):
        b, r = c // 4, c % 4
        return _ca(X[b, 1024 * r:1024 * (r + 1)])

    wqT = _ca(np.asarray(L0_attn_Wq, f32).T)
    wkT = _ca(np.asarray(L0_attn_Wk, f32).T)
    wvT = _ca(np.asarray(L0_attn_Wv, f32).T)
    idxT = _ca(np.asarray(L0_ffn_w_idx, np.int32).T)
    AM2 = _ca(np.asarray(L0_ffn_absmax, f32).reshape(4096, 64).T)
    ncA = build_qkv(0)
    maps = []
    for c in range(8):
        r = c % 4
        pos = np.arange(1024 * r, 1024 * (r + 1), dtype=np.float64)
        cos, sin = _rope_tabs(0, pos)
        maps.append(dict(x=shard(X0, c), wq=wqT, wk=wkT, wv=wvT, cos=cos,
                         sin=sin, perm=_perm(0), ident=IDENT,
                         idxs=_ca(idxT[:, 512 * c:512 * (c + 1)]),
                         am2s=_ca(AM2[:, 512 * c:512 * (c + 1)]), sel2=_sel2()))
    rA = _run(ncA, maps)
    WDQ = _ca(np.hstack([rA[c]["wdq"] for c in range(8)]))
    QFb = [np.hstack([rA[4 * b + r]["qfm"] for r in range(4)]) for b in range(2)]
    KFb = [np.hstack([rA[4 * b + r]["kfm"] for r in range(4)]) for b in range(2)]
    VTb = [_ca(np.hstack([rA[4 * b + r]["vfm"] for r in range(4)]).T)
           for b in range(2)]

    ncB = build_attn(True)
    maps = []
    for c in range(8):
        b, r = c // 4, c % 4
        maps.append(dict(qfm=_ca(QFb[b][1024 * r:1024 * (r + 1)]),
                         kfm=_ca(KFb[b][256 * r:256 * (r + 1)]),
                         vtm=_ca(VTb[b][:, 256 * r:256 * (r + 1)]),
                         diag=diag, ident=IDENT))
    rB = _run(ncB, maps)
    AOb = [np.vstack([rB[4 * b + r]["aofm"] for r in range(4)]) for b in range(2)]

    ncC = build_wo(True)
    woT = _ca(np.asarray(L0_attn_Wo, f32).T)
    fbias = _ca(np.asarray(L0_ffn_bias, f32).reshape(1, 4096))
    maps = []
    for c in range(8):
        b, r = c // 4, c % 4
        maps.append(dict(xin=shard(X0, c),
                         aofm=_ca(AOb[b][:, 1024 * r:1024 * (r + 1)]),
                         wo=woT, wdq=WDQ, fbias=fbias, ident=IDENT))
    rC = _run(ncC, maps)
    X2 = [rC[c]["xout"] for c in range(8)]

    ncD = build_qkv(1)
    wq1 = _ca(np.asarray(L1_attn_wq, f32))
    wk1 = _ca(np.asarray(L1_attn_wk, f32))
    wv1 = _ca(np.asarray(L1_attn_wv, f32))
    maps = []
    for c in range(8):
        r = c % 4
        pos = np.arange(1024 * r, 1024 * (r + 1), dtype=np.float64)
        cos, sin = _rope_tabs(1, pos)
        maps.append(dict(x=X2[c], wq=wq1, wk=wk1, wv=wv1, cos=cos, sin=sin,
                         perm=_perm(1), ident=IDENT))
    rD = _run(ncD, maps)
    QFb = [np.hstack([rD[4 * b + r]["qfm"] for r in range(4)]) for b in range(2)]
    KFb = [np.hstack([rD[4 * b + r]["kfm"] for r in range(4)]) for b in range(2)]
    VTb = [_ca(np.hstack([rD[4 * b + r]["vfm"] for r in range(4)]).T)
           for b in range(2)]

    ncE = build_attn(False)
    maps = []
    for c in range(8):
        b, r = c // 4, c % 4
        maps.append(dict(qfm=_ca(QFb[b][1024 * r:1024 * (r + 1)]),
                         kfm=_ca(KFb[b][256 * r:256 * (r + 1)]),
                         vtm=_ca(VTb[b][:, 256 * r:256 * (r + 1)]),
                         diag=diag, ident=IDENT))
    rE = _run(ncE, maps)
    AOb = [np.vstack([rE[4 * b + r]["aofm"] for r in range(4)]) for b in range(2)]

    ncF = build_wo(False)
    wo1 = _ca(np.asarray(L1_attn_wo, f32))
    maps = []
    for c in range(8):
        b, r = c // 4, c % 4
        maps.append(dict(xin=X2[c], aofm=_ca(AOb[b][:, 1024 * r:1024 * (r + 1)]),
                         wo=wo1, vmask=_vismask(c), ident=IDENT))
    rF = _run(ncF, maps)
    X3 = [rF[c]["xout"] for c in range(8)]
    H1 = [rF[c]["h1fm"] for c in range(8)]
    PTS = [rF[c]["parts"] for c in range(8)]

    ncG = build_moe()
    gw = np.asarray(L1_ffn_gate_w, f32)
    gwm = _ca(gw[:4096].reshape(32, 128, 3).transpose(1, 0, 2).reshape(128, 96))
    gtail = np.zeros((1, 8), f32)
    gtail[0, 0:3] = gw[4096]
    gtail[0, 3:6] = np.asarray(L1_ffn_gate_b, f32)
    w1 = _ca(np.asarray(L1_ffn_w1, f32))
    w2 = _ca(np.asarray(L1_ffn_w2, f32))
    b1 = np.asarray(L1_ffn_b1, f32)
    b1c = _ca(np.stack([b1[e].reshape(32, 128).T for e in range(3)]))
    b2r = _ca(np.asarray(L1_ffn_b2, f32))
    maps = []
    for c in range(8):
        b, r = c // 4, c % 4
        p4 = np.concatenate([PTS[4 * b + rr].reshape(2) for rr in range(4)])
        h0m = _ca(H1[4 * b][:, 0].reshape(32, 128).T)
        maps.append(dict(h1fm=H1[c], x3=X3[c], parts4=_ca(p4.reshape(1, 8)),
                         h0m=h0m, gwm=gwm, gtail=gtail, w1=w1, b1c=b1c,
                         w2=w2, b2r=b2r))
    rG = _run(ncG, maps)
    out = np.empty((2, 4096, 4096), f32)
    for c in range(8):
        b, r = c // 4, c % 4
        out[b, 1024 * r:1024 * (r + 1)] = rG[c]["x4"]
    return out
